# version 39
# baseline (speedup 1.0000x reference)
import numpy as np
import contextlib
import concourse.bass as bass
import concourse.mybir as mybir
from concourse.bass_utils import run_bass_kernel_spmd

F32 = mybir.dt.float32
BF16 = mybir.dt.bfloat16
I32 = mybir.dt.int32
ALU = mybir.AluOpType
ACTF = mybir.ActivationFunctionType
AX = mybir.AxisListType.X

B, S, D, H, DK = 4, 4096, 1024, 16, 64
NPAIR, NSLOT, NKT = 8, 4, 32
NS = [8, 16, 24, 32]
POS = {0: [1, 2, 5, 6], 1: [0, 3, 4, 7]}
EPS = 1e-5
QB = 127.0
ROPE_THETA = 10000.0


class Chain:
    def __init__(self, es, nc, name, step):
        self.h = es.enter_context(nc.semaphore(name))
        self.n = 0
        self.step = step


class Sched:

    def __init__(self, chains, keysems):
        self.steps = []
        self.ch = chains
        self.ks = keysems
        self.track = {}

    def add(self, eng, fn, reads=(), writes=(), after=(), dma=False):
        waits = {}

        def merge(h):
            if h is None:
                return
            e, n = h
            if n > waits.get(e, 0):
                waits[e] = n

        def merge_dep(h):
            if h is not None and not (eng == "T" and h[0] == "T"):
                merge(h)

        for h in after:
            merge(h)
        for k in reads:
            t = self.track.get(k)
            if t is not None:
                merge_dep(t[0])
        for k in writes:
            t = self.track.get(k)
            if t is not None:
                merge_dep(t[0])
                for r in t[1]:
                    merge_dep(r)
        if eng in ("G", "SY") or dma:
            assert len(writes) == 1, f"DMA step needs exactly one write key: {writes}"
            c = self.ks.get(writes[0])
            if c is None:
                c = self.ks[writes[0]] = self.ch["_mk"](writes[0])
            tagname = "K:" + writes[0]
        else:
            c = self.ch[eng]
            tagname = eng
        c.n += c.step
        h = (tagname, c.n)
        self.steps.append((eng, fn, sorted(waits.items()), tagname))
        for k in reads:
            self.track.setdefault(k, [None, []])[1].append(h)
        for k in writes:
            self.track[k] = [h, []]
        return h

    def run(self, block, barrier):
        chains = self.ch
        keysems = self.ks

        def sem_of(name):
            return keysems[name[2:]] if name.startswith("K:") else chains[name]

        def mk(name):
            def body(eng):
                waited = {}
                for e2, n2 in barrier.items():
                    if e2 != name and n2 > 0:
                        eng.wait_ge(sem_of(e2).h, n2)
                        waited[e2] = n2
                for e, fn, waits, tagname in self.steps:
                    if e != name:
                        continue
                    for we, wn in waits:
                        if wn > waited.get(we, 0):
                            eng.wait_ge(sem_of(we).h, wn)
                            waited[we] = wn
                    ins = fn(eng)
                    c = sem_of(tagname)
                    ins.then_inc(c.h, c.step)
            return body

        block.gpsimd(mk("G"))
        block.vector(mk("V"))
        block.tensor(mk("T"))
        block.scalar(mk("SC"))
        block.sync(mk("SY"))


def build_nc():
    nc = bass.Bass("TRN2", target_bir_lowering=False)
    ap = lambda t: t.ap()

    x_kv = ap(nc.dram_tensor("x_kv", [S, D], F32, kind="ExternalInput"))
    x_own = ap(nc.dram_tensor("x_own", [2048, D], F32, kind="ExternalInput"))
    wqkv = ap(nc.dram_tensor("wqkv", [3 * D, D], F32, kind="ExternalInput"))
    wo_in = ap(nc.dram_tensor("wo_in", [D, D], F32, kind="ExternalInput"))
    tern = ap(nc.dram_tensor("tern", [D, D], I32, kind="ExternalInput"))
    bias_in = ap(nc.dram_tensor("bias_in", [1, D], F32, kind="ExternalInput"))
    ascale = ap(nc.dram_tensor("ascale", [1, 1], F32, kind="ExternalInput"))
    cosb = ap(nc.dram_tensor("cosb", [128, S], F32, kind="ExternalInput"))
    sinb = ap(nc.dram_tensor("sinb", [128, S], F32, kind="ExternalInput"))
    cosq = ap(nc.dram_tensor("cosq", [128, 2048], F32, kind="ExternalInput"))
    sinq = ap(nc.dram_tensor("sinq", [128, 2048], F32, kind="ExternalInput"))
    pswap = ap(nc.dram_tensor("pswap", [128, 128], F32, kind="ExternalInput"))
    identm = ap(nc.dram_tensor("identm", [128, 128], F32, kind="ExternalInput"))
    masks = ap(nc.dram_tensor("masks", [2, 8, 128, 512], F32, kind="ExternalInput"))
    out = ap(nc.dram_tensor("out", [2048, D], F32, kind="ExternalOutput"))
    k_scr = ap(nc.dram_tensor("k_scr", [NPAIR, 128, S], BF16, kind="Internal"))
    xkv_bf = ap(nc.dram_tensor("xkv_bf", [S, D], BF16, kind="Internal"))
    xown_bf = ap(nc.dram_tensor("xown_bf", [2048, D], BF16, kind="Internal"))
    wqkv_bf = ap(nc.dram_tensor("wqkv_bf", [3 * D, D], BF16, kind="Internal"))
    wo_bf = ap(nc.dram_tensor("wo_bf", [D, D], BF16, kind="Internal"))
    v_scr = ap(nc.dram_tensor("v_scr", [NPAIR, 128, NKT, 256], BF16, kind="Internal"))
    q_scr = ap(nc.dram_tensor("q_scr", [NPAIR, 128, 2048], BF16, kind="Internal"))

    es = contextlib.ExitStack()
    with es:
        sb = lambda name, shape, dt: es.enter_context(nc.sbuf_tensor(name, shape, dt))

        identb = sb("identb", [128, 128], BF16)
        pswapb = sb("pswapb", [128, 128], BF16)
        ones_f = sb("ones_f", [128, 128], F32)
        sc_sb = sb("sc_sb", [1, 4], F32)
        sprime_bc = sb("sprime_bc", [128, 1], F32)
        rs_bc = sb("rs_bc", [128, 1], F32)
        onesb = sb("onesb", [1, 128], BF16)
        biasb = sb("biasb", [1, D], BF16)
        w_oT = sb("w_oT", [128, 8 * 1024], BF16)
        w_fT = sb("w_fT", [128, 8 * 1024], BF16)
        attn_n = sb("attn_n", [128, NPAIR * 2048], BF16)

        CH = {
            "T": Chain(es, nc, "tch", 1),
            "V": Chain(es, nc, "vch", 1),
            "SC": Chain(es, nc, "sch", 1),
        }
        KS = {}
        CH["_mk"] = lambda key: Chain(es, nc, "ks_" + key, 16)
        block = es.enter_context(nc.Block())

        def barrier_counts():
            d = {k: c.n for k, c in CH.items() if k != "_mk"}
            for k, c in KS.items():
                d["K:" + k] = c.n
            return d

        with contextlib.ExitStack() as ph:
            sbp = lambda name, shape, dt: ph.enter_context(nc.sbuf_tensor(name, shape, dt))
            psp = lambda name, shape, dt: ph.enter_context(nc.psum_tensor(name, shape, dt))
            w_st = [sbp(f"w_st{i}", [128, 1024], BF16) for i in range(2)]
            w_sti = sbp("w_sti", [128, 1024], I32)
            w_stf = sbp("w_stf", [128, 1024], F32)
            bias_row = sbp("bias_row", [1, D], F32)
            x_tm = [sbp(f"x_tm{i}", [128, 4096], BF16) for i in range(2)]
            w_kvT = sbp("w_kvT", [128, 8 * 2048], BF16)
            w_qT = sbp("w_qT", [128, 8 * 1024], BF16)
            x_T = [sbp(f"x_T{i}", [128, 8 * 512], BF16) for i in range(2)]
            kraw = [sbp(f"kraw{i}", [128, 512], BF16) for i in range(2)]
            ksw = [sbp(f"ksw{i}", [128, 512], BF16) for i in range(2)]
            t1b = sbp("t1b", [128, 512], BF16)
            t2b = sbp("t2b", [128, 512], BF16)
            krot = sbp("krot", [128, 512], BF16)
            v_stg = sbp("v_stg", [128, 2048], BF16)
            cs_b = [sbp(f"cs_b{i}", [128, 1024], BF16) for i in range(2)]
            pA = psp("pA", [128, 1024], F32)
            pB = psp("pB", [128, 1024], F32)
            pC = psp("pC", [128, 1024], F32)
            pD = psp("pD", [128, 1024], BF16)
            pE = psp("pE", [128, 1024], BF16)

            s = Sched(CH, KS)

            s.add("G", lambda g: g.dma_start(out=identb[:], in_=identm), writes=["identb"])
            s.add("G", lambda g: g.dma_start(out=pswapb[:], in_=pswap), writes=["pswapb"])
            s.add("G", lambda g: g.dma_start(out=bias_row[:], in_=bias_in), writes=["bias_row"])
            s.add("G", lambda g: g.dma_start(out=sc_sb[0:1, 0:1], in_=ascale), writes=["sc_sb"])
            s.add("V", lambda v: v.memset(ones_f[:], 1.0), writes=["ones_f"])
            s.add("V", lambda v: v.tensor_scalar(sc_sb[0:1, 1:2], sc_sb[0:1, 0:1], 1e-5, None,
                                                 op0=ALU.max), reads=["sc_sb"], writes=["sc_sb"])
            s.add("V", lambda v: v.reciprocal(sc_sb[0:1, 2:3], sc_sb[0:1, 1:2]),
                  reads=["sc_sb"], writes=["sc_sb"])
            s.add("V", lambda v: v.memset(onesb[:], 1.0), writes=["onesb"])
            s.add("V", lambda v: v.tensor_scalar(biasb[:], bias_row[:], sc_sb[0:1, 2:3], None,
                                                 op0=ALU.mult),
                  reads=["bias_row", "sc_sb"], writes=["biasb"])
            s.add("T", lambda t: t.matmul(pB[:, 0:1], ones_f[0:1, 0:128], sc_sb[0:1, 1:2],
                                          start=True, stop=True, skip_group_check=True),
                  reads=["ones_f", "sc_sb"], writes=["pB"])
            s.add("T", lambda t: t.matmul(pB[:, 1:2], ones_f[0:1, 0:128], sc_sb[0:1, 2:3],
                                          start=True, stop=True, skip_group_check=True),
                  reads=["ones_f", "sc_sb"], writes=["pB"])
            s.add("V", lambda v: v.tensor_copy(sprime_bc[:], pB[:, 0:1]),
                  reads=["pB"], writes=["sprime_bc"])
            s.add("V", lambda v: v.tensor_copy(rs_bc[:], pB[:, 1:2]),
                  reads=["pB"], writes=["rs_bc"])

            s.add("G", lambda g: g.dma_start(out=wqkv_bf, in_=wqkv), writes=["wqkv_bf"])
            s.add("G", lambda g: g.dma_start(out=wo_bf, in_=wo_in), writes=["wo_bf"])
            for c in range(8):
                s.add("SC", (lambda c: lambda e: e.dma_start_transpose(
                    w_kvT[:, c * 2048:(c + 1) * 2048],
                    wqkv_bf[D:3 * D, c * 128:(c + 1) * 128]))(c),
                    reads=["wqkv_bf"], writes=["w_kvT"], dma=True)
                s.add("SC", (lambda c: lambda e: e.dma_start_transpose(
                    w_qT[:, c * 1024:(c + 1) * 1024],
                    wqkv_bf[0:D, c * 128:(c + 1) * 128]))(c),
                    reads=["wqkv_bf"], writes=["w_qT"], dma=True)
                s.add("SC", (lambda c: lambda e: e.dma_start_transpose(
                    w_oT[:, c * 1024:(c + 1) * 1024],
                    wo_bf[:, c * 128:(c + 1) * 128]))(c),
                    reads=["wo_bf"], writes=["w_oT"], dma=True)

            wjobs = []
            for ot in range(8):
                wjobs.append((tern, ot * 128, w_fT, "w_fT", 1024, 0, ot, True))

            for ji, (src, row, dst, dkey, blk, base, ot, is_t) in enumerate(wjobs):
                stg = w_st[ji % 2]
                skey = f"w_st{ji % 2}"
                s.add("G", (lambda src, row: lambda g: g.dma_start(
                    out=w_sti[:], in_=src[row:row + 128, :]))(src, row),
                    writes=["w_sti"])
                s.add("SC", lambda sc: sc.copy(w_stf[:], w_sti[:]),
                      reads=["w_sti"], writes=["w_stf"])
                s.add("V", (lambda stg: lambda v: v.tensor_scalar(
                    stg[:], w_stf[:], -1.0, None, op0=ALU.add))(stg),
                    reads=["w_stf"], writes=[skey])
                pdst = pD if ji % 2 == 0 else pE
                pkey = "pD" if ji % 2 == 0 else "pE"
                for c in range(8):
                    s.add("T", (lambda pdst, stg, c: lambda t: t.transpose(
                        pdst[:, c * 128:(c + 1) * 128],
                        stg[:, c * 128:(c + 1) * 128], identb[:]))(pdst, stg, c),
                        reads=[skey, "identb"], writes=[pkey])
                dstv = dst[:].rearrange("p (ic c) -> p ic c", ic=8)[
                    :, :, base + ot * 128: base + (ot + 1) * 128]
                srcv = pdst[:, :].rearrange("p (ic c) -> p ic c", ic=8)
                s.add("V", (lambda dstv, srcv: lambda v: v.tensor_copy(dstv, srcv))(dstv, srcv),
                      reads=[pkey], writes=[dkey])

            def emit_xload(i):
                kind, idx = jobs[i]
                buf_i = i % 2
                src_ap = x_kv if kind == "K" else x_own
                xin = src_ap[idx * 512:(idx + 1) * 512, :].rearrange("(i p) c -> p i c", p=128)
                xdst = x_tm[buf_i][:].rearrange("p (i c) -> p i c", i=4)
                s.add("G", (lambda xin, xdst: lambda g: g.dma_start(out=xdst, in_=xin))(
                    xin, xdst), writes=[f"x_tm{buf_i}"])

            def emit_xtp_group(i, cp):
                buf_i = i % 2
                pdst = pD if cp % 2 == 0 else pE
                pkey = "pD" if cp % 2 == 0 else "pE"
                for ch in range(2):
                    c = 2 * cp + ch
                    for i4 in range(4):
                        s.add("T", (lambda pdst, b, i4, c, ch: lambda t: t.transpose(
                            pdst[:, ch * 512 + i4 * 128: ch * 512 + (i4 + 1) * 128],
                            x_tm[b][:, i4 * 1024 + c * 128: i4 * 1024 + (c + 1) * 128],
                            identb[:]))(pdst, buf_i, i4, c, ch),
                            reads=[f"x_tm{buf_i}", "identb"], writes=[pkey])
                s.add("SC", (lambda pdst, cp, b: lambda sc: sc.copy(
                    x_T[b][:, cp * 1024:(cp + 1) * 1024], pdst[:, :]))(pdst, cp, buf_i),
                    reads=[pkey], writes=[f"x_T{buf_i}"])

            def emit_xtp(i, _unused):
                for cp in range(4):
                    emit_xtp_group(i, cp)

            def emit_rope_proj(wT, wkey, p, xb):
                pk = pA if p % 2 == 0 else pB
                pkkey = "pA" if p % 2 == 0 else "pB"
                kr = kraw[p % 2]
                krk = f"kraw{p % 2}"
                for ic in range(8):
                    s.add("T", (lambda pk, wT, p, ic, xb: lambda t: t.matmul(
                        pk[:, 0:512], wT[:, ic * 2048 + p * 128: ic * 2048 + (p + 1) * 128]
                        if wT is w_kvT else wT[:, ic * 1024 + p * 128: ic * 1024 + (p + 1) * 128],
                        x_T[xb][:, ic * 512:(ic + 1) * 512],
                        start=(ic == 0), stop=(ic == 7), skip_group_check=True))(pk, wT, p, ic, xb),
                        reads=[wkey, f"x_T{xb}"], writes=[pkkey])
                s.add("SC", (lambda pk, kr: lambda sc: sc.copy(kr[:], pk[:, 0:512]))(pk, kr),
                      reads=[pkkey], writes=[krk])
                s.add("T", (lambda pk, kr: lambda t: t.matmul(
                    pk[:, 512:1024], pswapb[:], kr[:],
                    start=True, stop=True, skip_group_check=True))(pk, kr),
                    reads=["pswapb", krk], writes=[pkkey])
                s.add("SC", (lambda pk, p: lambda sc: sc.copy(
                    ksw[p % 2][:], pk[:, 512:1024]))(pk, p),
                    reads=[pkkey], writes=[f"ksw{p % 2}"])

            def emit_rope_tail(p, cs, cskey, dst_fn, dst_key):
                pk = pA if p % 2 == 0 else pB
                pkkey = "pA" if p % 2 == 0 else "pB"
                kr = kraw[p % 2]
                krk = f"kraw{p % 2}"
                s.add("V", (lambda cs, kr: lambda v: v.tensor_tensor(
                    t2b[:], kr[:], cs[:, 0:512], ALU.mult))(cs, kr),
                    reads=[krk, cskey], writes=["t2b"])
                s.add("V", (lambda p, cs: lambda v: v.tensor_tensor(
                    t1b[:], ksw[p % 2][:], cs[:, 512:1024], ALU.mult))(p, cs),
                    reads=[f"ksw{p % 2}", cskey], writes=["t1b"])
                s.add("V", lambda v: v.tensor_tensor(krot[:], t1b[:], t2b[:], ALU.add),
                      reads=["t1b", "t2b"], writes=["krot"])
                s.add("SY", dst_fn, reads=["krot"], writes=[dst_key])

            onesv0 = v_stg[:].rearrange("p (j e k) -> p j e k", j=8, e=2)[:, :, :, 64:128]
            s.add("V", (lambda onesv0: lambda v: v.memset(onesv0, 1.0))(onesv0),
                  reads=[], writes=["v_stg"])

            jobs = [("K", i) for i in range(8)] + [("Q", i) for i in range(4)]

            def emit_cs(kind, idx, bi):
                csl = cs_b[bi]
                cskey = f"cs_b{bi}"
                csrc, ssrc = (cosb, sinb) if kind == "K" else (cosq, sinq)
                s.add("G", (lambda idx, csl, csrc: lambda g: g.dma_start(
                    out=csl[:, 0:512], in_=csrc[:, idx * 512:(idx + 1) * 512]))(idx, csl, csrc),
                    writes=[cskey])
                s.add("G", (lambda idx, csl, ssrc: lambda g: g.dma_start(
                    out=csl[:, 512:1024], in_=ssrc[:, idx * 512:(idx + 1) * 512]))(idx, csl, ssrc),
                    writes=[cskey])


            def emit_vproj(t8, i4, bi):
                for ic in range(8):
                    for half in range(2):
                        s.add("T", (lambda i4, ic, half, b: lambda t: t.matmul(
                            pC[:, half * 512:(half + 1) * 512],
                            x_T[b][:, ic * 512 + i4 * 128: ic * 512 + (i4 + 1) * 128],
                            w_kvT[:, ic * 2048 + 1024 + half * 512:
                                  ic * 2048 + 1024 + (half + 1) * 512],
                            start=(ic == 0), stop=(ic == 7),
                            skip_group_check=True))(i4, ic, half, bi),
                            reads=[f"x_T{bi}", "w_kvT"], writes=["pC"])
                srcv = pC[:, :].rearrange("p (j e c) -> p j e c", j=8, e=2)
                dstv = v_stg[:].rearrange("p (j e k) -> p j e k", j=8, e=2)[:, :, :, 0:64]
                s.add("V", (lambda dstv, srcv: lambda v: v.tensor_copy(dstv, srcv))(
                    dstv, srcv), reads=["pC"], writes=["v_stg"])
                kt = t8 * 4 + i4
                vout = v_scr[:, :, kt, :].rearrange("j p c -> p j c")
                vsrc = v_stg[:].rearrange("p (j c) -> p j c", j=8)
                s.add("SY", (lambda vout, vsrc: lambda sy: sy.dma_start(
                    out=vout, in_=vsrc))(vout, vsrc), reads=["v_stg"], writes=["v_scr"])

            def emit_proc(i, nxt):
                kind, idx = jobs[i]
                bi = i % 2
                csl = cs_b[bi]
                cskey = f"cs_b{bi}"
                if kind == "K":
                    t8 = idx

                    def k_dst(p, t8):
                        return lambda sy: sy.dma_start(
                            out=k_scr[p, :, t8 * 512:(t8 + 1) * 512], in_=krot[:])
                    for p in range(8):
                        emit_rope_proj(w_kvT, "w_kvT", p, bi)
                        if nxt is not None and p % 2 == 1:
                            emit_xtp_group(nxt, p // 2)
                        if p >= 1:
                            emit_rope_tail(p - 1, csl, cskey, k_dst(p - 1, t8), "k_scr")
                        if p % 2 == 1:
                            emit_vproj(t8, p // 2, bi)
                    emit_rope_tail(7, csl, cskey, k_dst(7, t8), "k_scr")
                else:
                    u = idx

                    def q_dst(p, u):
                        return lambda sy: sy.dma_start(
                            out=q_scr[p, :, u * 512:(u + 1) * 512], in_=krot[:])
                    for p in range(8):
                        emit_rope_proj(w_qT, "w_qT", p, bi)
                        if nxt is not None and p % 2 == 1:
                            emit_xtp_group(nxt, p // 2)
                        if p >= 1:
                            emit_rope_tail(p - 1, csl, cskey, q_dst(p - 1, u), "q_scr")
                    emit_rope_tail(7, csl, cskey, q_dst(7, u), "q_scr")

            emit_xload(0)
            emit_cs(*jobs[0], 0)
            emit_xload(1)
            emit_xtp(0, None)
            for i in range(12):
                if i + 2 < 12:
                    emit_xload(i + 2)
                if i + 1 < 12:
                    emit_cs(*jobs[i + 1], (i + 1) % 2)
                    emit_proc(i, i + 1)
                else:
                    emit_proc(i, None)

            bar = barrier_counts()
            s.run(block, {})

        with contextlib.ExitStack() as ph:
            sbp = lambda name, shape, dt: ph.enter_context(nc.sbuf_tensor(name, shape, dt))
            psp = lambda name, shape, dt: ph.enter_context(nc.psum_tensor(name, shape, dt))
            maskb = sbp("maskb", [128, 8 * 512], BF16)
            k_run = [sbp(f"k_run{i}", [128, 4096], BF16) for i in range(2)]
            v_run = [sbp(f"v_run{i}", [128, NKT * 256], BF16) for i in range(2)]
            q_run = [sbp(f"q_run{i}", [128, 512], BF16) for i in range(2)]
            exp_b = [sbp(f"exp_b{i}", [128, 1024], BF16) for i in range(3)]
            recb = sbp("recb", [128, 1024], F32)
            pSC = [psp(f"pSC{i}", [128, 1024], F32) for i in range(2)]
            pAV = [psp(f"pAV{i}", [128, 1024], F32) for i in range(2)]
            ND = 4
            xo = [sbp(f"xo{i}", [128, 1024], F32) for i in range(2)]
            y1f = [sbp(f"y1f{i}", [128, 1024], F32) for i in range(ND)]
            ylnf = [sbp(f"ylnf{i}", [128, 1024], F32) for i in range(2)]
            uf = [sbp(f"uf{i}", [128, 1024], F32) for i in range(2)]
            uf2 = [sbp(f"uf2{i}", [128, 1024], F32) for i in range(2)]
            ui32 = [sbp("ui32", [128, 1024], I32)] * 2
            rdl = [sbp(f"rdl{i}", [128, 1024], F32) for i in range(2)]
            
            sqb = [sbp("sqb0", [128, 1024], F32)] * 2
            xq_bf = [sbp(f"xq_bf{i}", [128, 1024], BF16) for i in range(2)]
            xq_T = [sbp(f"xq_T{i}", [128, 1024], BF16) for i in range(2)]
            mst = sbp("mst", [128, 16 * ND], F32)


            s = Sched(CH, KS)

            def emit_stage(g16, st):
                sl, i4 = g16 // 4, g16 % 4
                bi = g16 % ND
                moff = 16 * bi
                if st == 0:
                    s.add("SY", (lambda g16, bi: lambda sy: sy.dma_start(
                        out=xo[bi % 2][:], in_=x_own[g16 * 128:(g16 + 1) * 128, :]))(g16, bi),
                        writes=[f"xo{bi % 2}"])
                    for dc in range(8):
                        for half in range(2):
                            s.add("T", (lambda sl, i4, dc, half: lambda te: te.matmul(
                                pSC[1][:, half * 512:(half + 1) * 512],
                                attn_n[:, dc * 2048 + sl * 512 + i4 * 128:
                                       dc * 2048 + sl * 512 + (i4 + 1) * 128],
                                w_oT[:, dc * 1024 + half * 512: dc * 1024 + (half + 1) * 512],
                                start=(dc == 0), stop=(dc == 7),
                                skip_group_check=True))(sl, i4, dc, half),
                                reads=["attn_n", "w_oT"], writes=["pSC1"])
                elif st == 1:
                    s.add("V", (lambda bi: lambda v: v.tensor_tensor(
                        y1f[bi][:], pSC[1][:, :], xo[bi % 2][:], ALU.add))(bi),
                        reads=["pSC1", f"xo{bi % 2}"], writes=[f"y1f{bi}"])
                elif st == 2:
                    s.add("SC", (lambda bi, moff: lambda sc: sc.activation(
                        sqb[bi % 2][:], y1f[bi][:], ACTF.Identity, scale=1.0 / D,
                        accum_out=mst[:, moff + 2:moff + 3]))(bi, moff),
                        reads=[f"y1f{bi}"], writes=["sqb", f"mst{bi}"])
                    s.add("SC", (lambda bi, moff: lambda sc: sc.activation(
                        sqb[bi % 2][:], y1f[bi][:], ACTF.Square, scale=1.0 / float(np.sqrt(D)),
                        accum_out=mst[:, moff + 3:moff + 4]))(bi, moff),
                        reads=[f"y1f{bi}"], writes=["sqb", f"mst{bi}"])
                elif st == 3:
                    s.add("V", (lambda bi, moff: lambda v: v.scalar_tensor_tensor(
                        mst[:, moff + 4:moff + 5], mst[:, moff + 2:moff + 3],
                        mst[:, moff + 2:moff + 3], mst[:, moff + 3:moff + 4],
                        op0=ALU.mult, op1=ALU.subtract))(bi, moff),
                        reads=[f"mst{bi}"], writes=[f"mst{bi}"])
                    s.add("V", (lambda bi, moff: lambda v: v.tensor_scalar(
                        mst[:, moff + 4:moff + 5], mst[:, moff + 4:moff + 5], -1.0, EPS,
                        op0=ALU.mult, op1=ALU.add))(bi, moff),
                        reads=[f"mst{bi}"], writes=[f"mst{bi}"])
                elif st == 4:
                    s.add("SC", (lambda bi, moff: lambda sc: sc.activation(
                        mst[:, moff + 5:moff + 6], mst[:, moff + 4:moff + 5],
                        ACTF.Ln))(bi, moff),
                        reads=[f"mst{bi}"], writes=[f"mst{bi}"])
                    s.add("SC", (lambda bi, moff: lambda sc: sc.activation(
                        mst[:, moff + 6:moff + 7], mst[:, moff + 5:moff + 6],
                        ACTF.Exp, scale=-0.5))(bi, moff),
                        reads=[f"mst{bi}"], writes=[f"mst{bi}"])
                elif st == 5:
                    s.add("V", (lambda bi, moff: lambda v: v.tensor_scalar(
                        mst[:, moff + 7:moff + 8], mst[:, moff + 2:moff + 3],
                        mst[:, moff + 6:moff + 7], -1.0,
                        op0=ALU.mult, op1=ALU.mult))(bi, moff),
                        reads=[f"mst{bi}"], writes=[f"mst{bi}"])
                    s.add("V", (lambda bi, moff: lambda v: v.tensor_scalar(
                        ylnf[bi % 2][:], y1f[bi][:], mst[:, moff + 6:moff + 7],
                        mst[:, moff + 7:moff + 8], op0=ALU.mult, op1=ALU.add))(bi, moff),
                        reads=[f"y1f{bi}", f"mst{bi}"], writes=[f"ylnf{bi % 2}"])
                    s.add("V", (lambda bi: lambda v: v.tensor_scalar(
                        uf[bi % 2][:], ylnf[bi % 2][:], rs_bc[:, 0:1], None, op0=ALU.mult))(bi),
                        reads=[f"ylnf{bi % 2}", "rs_bc"], writes=[f"uf{bi % 2}"])
                elif st == 6:
                    s.add("SC", (lambda bi: lambda sc: sc.copy(ui32[bi % 2][:], uf[bi % 2][:]))(bi),
                          reads=[f"uf{bi % 2}"], writes=["ui32"])
                    s.add("SC", (lambda bi: lambda sc: sc.copy(uf2[bi % 2][:], ui32[bi % 2][:]))(bi),
                          reads=["ui32"], writes=[f"uf2{bi % 2}"])
                elif st == 7:
                    s.add("V", (lambda bi: lambda v: v.tensor_tensor(
                        rdl[bi % 2][:], uf[bi % 2][:], uf2[bi % 2][:], ALU.subtract))(bi),
                        reads=[f"uf{bi % 2}", f"uf2{bi % 2}"], writes=[f"rdl{bi % 2}"])
                    s.add("V", (lambda bi: lambda v: v.scalar_tensor_tensor(
                        uf[bi % 2][:], rdl[bi % 2][:], 0.5, uf2[bi % 2][:], op0=ALU.is_ge,
                        op1=ALU.add))(bi),
                        reads=[f"rdl{bi % 2}", f"uf2{bi % 2}"], writes=[f"uf{bi % 2}"])
                    s.add("V", (lambda bi: lambda v: v.tensor_scalar(
                        rdl[bi % 2][:], rdl[bi % 2][:], -0.5, -1.0, op0=ALU.is_le,
                        op1=ALU.mult))(bi),
                        reads=[f"rdl{bi % 2}"], writes=[f"rdl{bi % 2}"])
                    s.add("V", (lambda bi: lambda v: v.tensor_tensor(
                        uf[bi % 2][:], uf[bi % 2][:], rdl[bi % 2][:], ALU.add))(bi),
                        reads=[f"uf{bi % 2}", f"rdl{bi % 2}"], writes=[f"uf{bi % 2}"])
                    s.add("V", (lambda bi: lambda v: v.tensor_scalar(
                        xq_bf[bi % 2][:], uf[bi % 2][:], QB, -QB, op0=ALU.min, op1=ALU.max))(bi),
                        reads=[f"uf{bi % 2}"], writes=[f"xq_bf{bi % 2}"])
                elif st == 8:
                    for c in range(8):
                        s.add("SC", (lambda c, bi: lambda e: e.dma_start_transpose(
                            xq_T[bi % 2][:, c * 128:(c + 1) * 128],
                            xq_bf[bi % 2][:, c * 128:(c + 1) * 128]))(c, bi),
                            reads=[f"xq_bf{bi % 2}"], writes=[f"xq_T{bi % 2}"], dma=True)
                elif st == 9:
                    for ic in range(8):
                        for half in range(2):
                            s.add("T", (lambda ic, half, bi: lambda te: te.matmul(
                                pSC[1][:, half * 512:(half + 1) * 512],
                                xq_T[bi % 2][:, ic * 128:(ic + 1) * 128],
                                w_fT[:, ic * 1024 + half * 512: ic * 1024 + (half + 1) * 512],
                                start=(ic == 0), stop=False,
                                skip_group_check=True))(ic, half, bi),
                                reads=[f"xq_T{bi % 2}", "w_fT"], writes=["pSC1"])
                    for half in range(2):
                        s.add("T", (lambda half: lambda te: te.matmul(
                            pSC[1][:, half * 512:(half + 1) * 512],
                            onesb[0:1, :], biasb[0:1, half * 512:(half + 1) * 512],
                            start=False, stop=True, skip_group_check=True))(half),
                            reads=["onesb", "biasb"], writes=["pSC1"])
                elif st == 10:
                    s.add("V", (lambda bi: lambda v: v.scalar_tensor_tensor(
                        y1f[bi][:], pSC[1][:, :], sprime_bc[:, 0:1], ylnf[bi % 2][:],
                        op0=ALU.mult, op1=ALU.add))(bi),
                        reads=["pSC1", "sprime_bc", f"ylnf{bi % 2}"], writes=[f"y1f{bi}"])
                elif st == 11:
                    s.add("SC", (lambda bi, moff: lambda sc: sc.activation(
                        sqb[bi % 2][:], y1f[bi][:], ACTF.Identity, scale=1.0 / D,
                        accum_out=mst[:, moff + 10:moff + 11]))(bi, moff),
                        reads=[f"y1f{bi}"], writes=["sqb", f"mst{bi}"])
                    s.add("SC", (lambda bi, moff: lambda sc: sc.activation(
                        sqb[bi % 2][:], y1f[bi][:], ACTF.Square, scale=1.0 / float(np.sqrt(D)),
                        accum_out=mst[:, moff + 11:moff + 12]))(bi, moff),
                        reads=[f"y1f{bi}"], writes=["sqb", f"mst{bi}"])
                elif st == 12:
                    s.add("V", (lambda bi, moff: lambda v: v.scalar_tensor_tensor(
                        mst[:, moff + 12:moff + 13], mst[:, moff + 10:moff + 11],
                        mst[:, moff + 10:moff + 11], mst[:, moff + 11:moff + 12],
                        op0=ALU.mult, op1=ALU.subtract))(bi, moff),
                        reads=[f"mst{bi}"], writes=[f"mst{bi}"])
                    s.add("V", (lambda bi, moff: lambda v: v.tensor_scalar(
                        mst[:, moff + 12:moff + 13], mst[:, moff + 12:moff + 13], -1.0, EPS,
                        op0=ALU.mult, op1=ALU.add))(bi, moff),
                        reads=[f"mst{bi}"], writes=[f"mst{bi}"])
                elif st == 13:
                    s.add("SC", (lambda bi, moff: lambda sc: sc.activation(
                        mst[:, moff + 13:moff + 14], mst[:, moff + 12:moff + 13],
                        ACTF.Ln))(bi, moff),
                        reads=[f"mst{bi}"], writes=[f"mst{bi}"])
                    s.add("SC", (lambda bi, moff: lambda sc: sc.activation(
                        mst[:, moff + 14:moff + 15], mst[:, moff + 13:moff + 14],
                        ACTF.Exp, scale=-0.5))(bi, moff),
                        reads=[f"mst{bi}"], writes=[f"mst{bi}"])
                elif st == 14:
                    s.add("V", (lambda bi, moff: lambda v: v.tensor_scalar(
                        mst[:, moff + 15:moff + 16], mst[:, moff + 10:moff + 11],
                        mst[:, moff + 14:moff + 15], -1.0,
                        op0=ALU.mult, op1=ALU.mult))(bi, moff),
                        reads=[f"mst{bi}"], writes=[f"mst{bi}"])
                    s.add("V", (lambda bi, moff: lambda v: v.tensor_scalar(
                        xo[bi % 2][:], y1f[bi][:], mst[:, moff + 14:moff + 15],
                        mst[:, moff + 15:moff + 16], op0=ALU.mult, op1=ALU.add))(bi, moff),
                        reads=[f"y1f{bi}", f"mst{bi}"], writes=[f"xo{bi % 2}"])
                elif st == 15:
                    s.add("SY", (lambda g16, bi: lambda sy: sy.dma_start(
                        out=out[g16 * 128:(g16 + 1) * 128, :], in_=xo[bi % 2][:]))(g16, bi),
                        reads=[f"xo{bi % 2}"], writes=["out"])


            NSTAGE, OFF = 16, 4
            ep_k_holder = [0]

            iters = [(sl, j) for sl in range(NSLOT) for j in range(NPAIR)]
            pending_tail = None
            for it, (sl, j) in enumerate(iters):
                n = NS[sl]
                bi = it % 2
                if j == 0:
                    mdst = maskb[:].rearrange("p (b c) -> p b c", b=8)
                    msrc = masks[sl % 2, :, :, :].rearrange("b p c -> p b c")
                    s.add("G", (lambda mdst, msrc: lambda g: g.dma_start(
                        out=mdst, in_=msrc))(mdst, msrc), writes=["maskb"])
                kk, vk, qk = f"k_run{bi}", f"v_run{bi}", f"q_run{bi}"
                s.add("SY", (lambda j, n, bi: lambda sy: sy.dma_start(
                    out=k_run[bi][:, 0:n * 128], in_=k_scr[j, :, 0:n * 128]))(j, n, bi),
                    reads=["k_scr"], writes=[kk])
                vin = v_scr[j, :, 0:n, :]
                vdst = v_run[bi][:, 0:n * 256].rearrange("p (a c) -> p a c", a=n)
                s.add("SY", (lambda vin, vdst: lambda sy: sy.dma_start(
                    out=vdst, in_=vin))(vin, vdst), reads=["v_scr"], writes=[vk])
                s.add("SY", (lambda j, sl, bi: lambda sy: sy.dma_start(
                    out=q_run[bi][:], in_=q_scr[j, :, sl * 512:(sl + 1) * 512]))(j, sl, bi),
                    reads=["q_scr"], writes=[qk])
                def emit_scores(t):
                    sb_ = (t % 2) if sl < 3 else 0
                    pk = f"pSC{sb_}"
                    s.add("T", (lambda sb_, bi, t: lambda te: te.matmul(
                        pSC[sb_][:, 0:512], k_run[bi][0:64, t * 128:(t + 1) * 128],
                        q_run[bi][0:64, :], start=True, stop=True,
                        tile_position=(0, 0), skip_group_check=True))(sb_, bi, t),
                        reads=[kk, qk], writes=[pk])
                    s.add("T", (lambda sb_, bi, t: lambda te: te.matmul(
                        pSC[sb_][:, 512:1024],
                        k_run[bi][64:128, t * 128:(t + 1) * 128],
                        q_run[bi][64:128, :], start=True, stop=True,
                        tile_position=(64, 0), skip_group_check=True))(sb_, bi, t),
                        reads=[kk, qk], writes=[pk])

                def emit_exp(t):
                    eb = t % 3
                    sb_ = (t % 2) if sl < 3 else 0
                    s.add("SC", (lambda eb, sb_: lambda sc: sc.activation(
                        exp_b[eb][:, :], pSC[sb_][:, :], ACTF.Exp))(eb, sb_),
                        reads=[f"pSC{sb_}"], writes=[f"exp_b{eb}"])

                def emit_mask(t):
                    if t < n - 8:
                        return
                    eb = t % 3
                    w = t - (n - 8)
                    moff = w * 512
                    for half in range(2):
                        s.add("V", (lambda eb, moff, half: lambda v: v.tensor_tensor(
                            exp_b[eb][:, half * 512:(half + 1) * 512],
                            exp_b[eb][:, half * 512:(half + 1) * 512],
                            maskb[:, moff:moff + 512], ALU.mult))(eb, moff, half),
                            reads=[f"exp_b{eb}", "maskb"], writes=[f"exp_b{eb}"])

                def emit_av(t):
                    eb = t % 3
                    ek = f"exp_b{eb}"
                    pit = it % 2
                    for hh in range(2):
                        s.add("T", (lambda eb, bi, t, hh, n, pit: lambda te: te.matmul(
                            pAV[pit][:, hh * 512:(hh + 1) * 512],
                            v_run[bi][:, t * 256 + hh * 128: t * 256 + (hh + 1) * 128],
                            exp_b[eb][:, hh * 512:(hh + 1) * 512],
                            start=(t == 0), stop=(t == n - 1),
                            skip_group_check=True))(eb, bi, t, hh, n, pit),
                            reads=[ek, vk], writes=[f"pAV{pit}"])

                def emit_tail(psl, pj, pit):
                    s.add("SC", (lambda pit: lambda sc: sc.activation(
                        recb[64:128, :], pAV[pit][64:128, :], ACTF.Ln))(pit),
                        reads=[f"pAV{pit}"], writes=["recb"])
                    s.add("SC", lambda sc: sc.activation(
                        recb[64:128, :], recb[64:128, :], ACTF.Exp, scale=-1.0),
                        reads=["recb"], writes=["recb"])
                    dstA = attn_n[0:64, pj * 2048 + psl * 512: pj * 2048 + (psl + 1) * 512]
                    s.add("V", (lambda dstA, pit: lambda v: v.tensor_tensor(
                        dstA, pAV[pit][0:64, 0:512], recb[64:128, 0:512], ALU.mult))(dstA, pit),
                        reads=[f"pAV{pit}", "recb"], writes=["attn_n"])
                    dstB = attn_n[64:128, pj * 2048 + psl * 512: pj * 2048 + (psl + 1) * 512]
                    s.add("V", (lambda dstB, pit: lambda v: v.tensor_tensor(
                        dstB, pAV[pit][0:64, 512:1024], recb[64:128, 512:1024],
                        ALU.mult))(dstB, pit),
                        reads=[f"pAV{pit}", "recb"], writes=["attn_n"])

                for t in range(n):
                    emit_scores(t)
                    emit_exp(t)
                    emit_mask(t)
                    if t >= 2:
                        emit_av(t - 2)
                    if t == 4 and pending_tail is not None:
                        emit_tail(*pending_tail)
                        pending_tail = None
                emit_av(n - 2)
                emit_av(n - 1)
                if pending_tail is not None:
                    emit_tail(*pending_tail)
                    pending_tail = None
                pending_tail = (sl, j, it % 2)
                if sl == 3:
                    target = (j + 1) * 8
                    while ep_k_holder[0] < target:
                        for g16 in reversed(range(12)):
                            st = ep_k_holder[0] - OFF * g16
                            if 0 <= st < NSTAGE:
                                emit_stage(g16, st)
                        ep_k_holder[0] += 1

            for k in range(ep_k_holder[0], NSTAGE + OFF * 11 + 1):
                for g16 in reversed(range(12)):
                    st = k - OFF * g16
                    if 0 <= st < NSTAGE:
                        emit_stage(g16, st)
            if pending_tail is not None:
                emit_tail(*pending_tail)
                pending_tail = None
            for k in range(NSTAGE + OFF * 3 + 1):
                for g16 in reversed(range(12, 16)):
                    st = k - OFF * (g16 - 12)
                    if 0 <= st < NSTAGE:
                        emit_stage(g16, st)
            s.run(block, bar)
            bar = barrier_counts()
            @block.sync
            def _(sy):
                sy.wait_ge(KS["out"].h, KS["out"].n)


    return nc



def _host_tables(cls):
    j = np.arange(128) % 32
    invf = ROPE_THETA ** (-(j.astype(np.float64)) / 32.0)
    srange = np.arange(S, dtype=np.float64)
    ang = srange[None, :] * invf[:, None]
    amp = 1.0 / np.sqrt(np.sqrt(64.0))
    cos_t = (np.cos(ang) * amp).astype(np.float32)
    sgn = np.where((np.arange(128) % 64) < 32, -1.0, 1.0) * amp
    sin_t = (np.sin(ang) * sgn[:, None]).astype(np.float32)
    pos = POS[cls]
    qcols = np.concatenate([np.arange(512 * p, 512 * (p + 1)) for p in pos])
    cosq_t = cos_t[:, qcols].copy()
    sinq_t = sin_t[:, qcols].copy()
    kk = np.arange(128)[:, None]
    qq = np.arange(512)[None, :]
    tri = [(kk + 128 * d <= qq).astype(np.float32) for d in range(4)]
    O = np.ones((128, 512), np.float32)
    Z = np.zeros((128, 512), np.float32)
    OT = np.stack([O, O, O, O] + tri)
    TZ = np.stack(tri + [Z, Z, Z, Z])
    if cls == 0:
        mk = np.stack([OT, TZ])
    else:
        mk = np.stack([TZ, OT])
    return cos_t, sin_t, cosq_t, sinq_t, mk


def kernel(L0_attn_x, L0_attn_W_qkv, L0_attn_W_o, L0_ffn_ternary_weight,
           L0_ffn_bias, L0_ffn_act_scale):
    x = np.ascontiguousarray(np.asarray(L0_attn_x, dtype=np.float32))
    wq = np.asarray(L0_attn_W_qkv, dtype=np.float32)
    wo = np.ascontiguousarray(np.asarray(L0_attn_W_o, dtype=np.float32))
    tw = np.ascontiguousarray(np.asarray(L0_ffn_ternary_weight, dtype=np.int32))
    bb = np.asarray(L0_ffn_bias, dtype=np.float32).reshape(1, D)
    asc = np.asarray(L0_ffn_act_scale, dtype=np.float32).reshape(1, 1)

    perm = np.arange(3 * D)
    for blk in range(2):
        for h in range(H):
            for e in range(2):
                for jj in range(32):
                    perm[blk * D + h * 64 + e * 32 + jj] = blk * D + h * 64 + 2 * jj + e
    wqp = np.ascontiguousarray(wq[perm])

    pswap_m = np.zeros((128, 128), np.float32)
    for h2 in range(2):
        for jj in range(32):
            pswap_m[h2 * 64 + jj, h2 * 64 + 32 + jj] = 1.0
            pswap_m[h2 * 64 + 32 + jj, h2 * 64 + jj] = 1.0
    ident = np.eye(128, dtype=np.float32)

    nc = build_nc()
    in_maps = []
    metas = []
    for core in range(8):
        b, cls = core // 2, core % 2
        cos_t, sin_t, cosq_t, sinq_t, mk = _host_tables(cls)
        pos = POS[cls]
        x_ownc = np.ascontiguousarray(
            np.concatenate([x[b, 512 * p:512 * (p + 1), :] for p in pos], axis=0))
        in_maps.append({
            "x_kv": np.ascontiguousarray(x[b]),
            "x_own": x_ownc,
            "wqkv": wqp,
            "wo_in": wo,
            "tern": tw,
            "bias_in": bb,
            "ascale": asc,
            "cosb": cos_t,
            "sinb": sin_t,
            "cosq": np.ascontiguousarray(cosq_t),
            "sinq": np.ascontiguousarray(sinq_t),
            "pswap": pswap_m,
            "identm": ident,
            "masks": np.ascontiguousarray(mk),
        })
        metas.append((b, pos))

    res = run_bass_kernel_spmd(nc, in_maps, core_ids=list(range(8)))
    outp = np.empty((B, S, D), dtype=np.float32)
    for core in range(8):
        b, pos = metas[core]
        oc = res.results[core]["out"]
        for si, p in enumerate(pos):
            outp[b, 512 * p:512 * (p + 1), :] = oc[512 * si:512 * (si + 1), :]
    return outp


# version 41
# speedup vs baseline: 1.3908x; 1.3908x over previous
import numpy as np
import contextlib
import concourse.bass as bass
import concourse.mybir as mybir
from concourse.bass_utils import run_bass_kernel_spmd

F32 = mybir.dt.float32
BF16 = mybir.dt.bfloat16
I32 = mybir.dt.int32
ALU = mybir.AluOpType
ACTF = mybir.ActivationFunctionType
AX = mybir.AxisListType.X

B, S, D, H, DK = 4, 4096, 1024, 16, 64
NPAIR, NSLOT, NKT = 8, 4, 32
NS = [8, 16, 24, 32]
POS = {0: [1, 2, 5, 6], 1: [0, 3, 4, 7]}
EPS = 1e-5
QB = 127.0
ROPE_THETA = 10000.0


class Chain:
    def __init__(self, es, nc, name, step):
        self.h = es.enter_context(nc.semaphore(name))
        self.n = 0
        self.step = step


class Sched:

    def __init__(self, chains, keysems):
        self.steps = []
        self.ch = chains
        self.ks = keysems
        self.track = {}

    def add(self, eng, fn, reads=(), writes=(), after=(), dma=False):
        waits = {}

        def merge(h):
            if h is None:
                return
            e, n = h
            if n > waits.get(e, 0):
                waits[e] = n

        def merge_dep(h):
            if h is not None and not (eng == "T" and h[0] == "T"):
                merge(h)

        for h in after:
            merge(h)
        for k in reads:
            t = self.track.get(k)
            if t is not None:
                merge_dep(t[0])
        for k in writes:
            t = self.track.get(k)
            if t is not None:
                merge_dep(t[0])
                for r in t[1]:
                    merge_dep(r)
        if eng in ("G", "SY") or dma:
            assert len(writes) == 1, f"DMA step needs exactly one write key: {writes}"
            c = self.ks.get(writes[0])
            if c is None:
                c = self.ks[writes[0]] = self.ch["_mk"](writes[0])
            tagname = "K:" + writes[0]
        else:
            c = self.ch[eng]
            tagname = eng
        c.n += c.step
        h = (tagname, c.n)
        self.steps.append((eng, fn, sorted(waits.items()), tagname))
        for k in reads:
            self.track.setdefault(k, [None, []])[1].append(h)
        for k in writes:
            self.track[k] = [h, []]
        return h

    def run(self, block, barrier):
        chains = self.ch
        keysems = self.ks

        def sem_of(name):
            return keysems[name[2:]] if name.startswith("K:") else chains[name]

        def mk(name):
            def body(eng):
                waited = {}
                for e2, n2 in barrier.items():
                    if e2 != name and n2 > 0:
                        eng.wait_ge(sem_of(e2).h, n2)
                        waited[e2] = n2
                for e, fn, waits, tagname in self.steps:
                    if e != name:
                        continue
                    for we, wn in waits:
                        if wn > waited.get(we, 0):
                            eng.wait_ge(sem_of(we).h, wn)
                            waited[we] = wn
                    ins = fn(eng)
                    c = sem_of(tagname)
                    ins.then_inc(c.h, c.step)
            return body

        block.gpsimd(mk("G"))
        block.vector(mk("V"))
        block.tensor(mk("T"))
        block.scalar(mk("SC"))
        block.sync(mk("SY"))


def build_nc():
    nc = bass.Bass("TRN2", target_bir_lowering=False)
    ap = lambda t: t.ap()

    x_kv = ap(nc.dram_tensor("x_kv", [S, D], F32, kind="ExternalInput"))
    x_own = ap(nc.dram_tensor("x_own", [2048, D], F32, kind="ExternalInput"))
    wqkv = ap(nc.dram_tensor("wqkv", [3 * D, D], F32, kind="ExternalInput"))
    wo_in = ap(nc.dram_tensor("wo_in", [D, D], F32, kind="ExternalInput"))
    tern = ap(nc.dram_tensor("tern", [D, D], I32, kind="ExternalInput"))
    bias_in = ap(nc.dram_tensor("bias_in", [1, D], F32, kind="ExternalInput"))
    ascale = ap(nc.dram_tensor("ascale", [1, 1], F32, kind="ExternalInput"))
    cosb = ap(nc.dram_tensor("cosb", [128, S], F32, kind="ExternalInput"))
    sinb = ap(nc.dram_tensor("sinb", [128, S], F32, kind="ExternalInput"))
    cosq = ap(nc.dram_tensor("cosq", [128, 2048], F32, kind="ExternalInput"))
    sinq = ap(nc.dram_tensor("sinq", [128, 2048], F32, kind="ExternalInput"))
    pswap = ap(nc.dram_tensor("pswap", [128, 128], F32, kind="ExternalInput"))
    identm = ap(nc.dram_tensor("identm", [128, 128], F32, kind="ExternalInput"))
    masks = ap(nc.dram_tensor("masks", [2, 8, 128, 512], F32, kind="ExternalInput"))
    out = ap(nc.dram_tensor("out", [2048, D], F32, kind="ExternalOutput"))
    k_scr = ap(nc.dram_tensor("k_scr", [NPAIR, 128, S], BF16, kind="Internal"))
    xkv_bf = ap(nc.dram_tensor("xkv_bf", [S, D], BF16, kind="Internal"))
    xown_bf = ap(nc.dram_tensor("xown_bf", [2048, D], BF16, kind="Internal"))
    wqkv_bf = ap(nc.dram_tensor("wqkv_bf", [3 * D, D], BF16, kind="Internal"))
    wo_bf = ap(nc.dram_tensor("wo_bf", [D, D], BF16, kind="Internal"))
    v_scr = ap(nc.dram_tensor("v_scr", [NPAIR, 128, NKT, 256], BF16, kind="Internal"))
    q_scr = ap(nc.dram_tensor("q_scr", [NPAIR, 128, 2048], BF16, kind="Internal"))

    es = contextlib.ExitStack()
    with es:
        sb = lambda name, shape, dt: es.enter_context(nc.sbuf_tensor(name, shape, dt))

        identb = sb("identb", [128, 128], BF16)
        pswapb = sb("pswapb", [128, 128], BF16)
        ones_f = sb("ones_f", [128, 128], F32)
        bias_bc = sb("bias_bc", [128, D], F32)
        sc_sb = sb("sc_sb", [1, 4], F32)
        sprime_bc = sb("sprime_bc", [128, 1], F32)
        rs_bc = sb("rs_bc", [128, 1], F32)
        onesb = sb("onesb", [1, 128], BF16)
        biasb = sb("biasb", [1, D], BF16)
        w_oT = sb("w_oT", [128, 8 * 1024], BF16)
        w_fT = sb("w_fT", [128, 8 * 1024], BF16)
        attn_n = sb("attn_n", [128, NPAIR * 2048], BF16)

        CH = {
            "T": Chain(es, nc, "tch", 1),
            "V": Chain(es, nc, "vch", 1),
            "SC": Chain(es, nc, "sch", 1),
        }
        KS = {}
        CH["_mk"] = lambda key: Chain(es, nc, "ks_" + key, 16)
        block = es.enter_context(nc.Block())

        def barrier_counts():
            d = {k: c.n for k, c in CH.items() if k != "_mk"}
            for k, c in KS.items():
                d["K:" + k] = c.n
            return d

        with contextlib.ExitStack() as ph:
            sbp = lambda name, shape, dt: ph.enter_context(nc.sbuf_tensor(name, shape, dt))
            psp = lambda name, shape, dt: ph.enter_context(nc.psum_tensor(name, shape, dt))
            w_st = [sbp(f"w_st{i}", [128, 1024], BF16) for i in range(2)]
            w_sti = [sbp(f"w_sti{i}", [128, 1024], I32) for i in range(2)]
            w_stf = sbp("w_stf", [128, 1024], F32)
            bias_row = sbp("bias_row", [1, D], F32)
            x_tm = [sbp(f"x_tm{i}", [128, 4096], BF16) for i in range(3)]
            w_kvT = sbp("w_kvT", [128, 8 * 2048], BF16)
            w_qT = sbp("w_qT", [128, 8 * 1024], BF16)
            x_T = [sbp(f"x_T{i}", [128, 8 * 512], BF16) for i in range(2)]
            kraw = [sbp(f"kraw{i}", [128, 512], BF16) for i in range(3)]
            ksw = [sbp(f"ksw{i}", [128, 512], BF16) for i in range(2)]
            t1b = sbp("t1b", [128, 512], BF16)
            t2b = sbp("t2b", [128, 512], BF16)
            krot = [sbp(f"krot{i}", [128, 512], BF16) for i in range(2)]
            v_stg = [sbp(f"v_stg{i}", [128, 2048], BF16) for i in range(2)]
            cs_b = [sbp(f"cs_b{i}", [128, 1024], BF16) for i in range(2)]
            pRaw = [psp(f"pRaw{i}", [128, 512], F32) for i in range(3)]
            pSwp = psp("pSwp", [128, 512], F32)
            pC = [psp(f"pC{i}", [128, 512], F32) for i in range(2)]
            pD = psp("pD", [128, 1024], BF16)
            pE = psp("pE", [128, 1024], BF16)

            s = Sched(CH, KS)

            s.add("G", lambda g: g.dma_start(out=identb[:], in_=identm), writes=["identb"])
            s.add("G", lambda g: g.dma_start(out=pswapb[:], in_=pswap), writes=["pswapb"])
            s.add("G", lambda g: g.dma_start(out=bias_row[:], in_=bias_in), writes=["bias_row"])
            s.add("G", lambda g: g.dma_start(out=sc_sb[0:1, 0:1], in_=ascale), writes=["sc_sb"])
            s.add("V", lambda v: v.memset(ones_f[:], 1.0), writes=["ones_f"])
            s.add("V", lambda v: v.tensor_scalar(sc_sb[0:1, 1:2], sc_sb[0:1, 0:1], 1e-5, None,
                                                 op0=ALU.max), reads=["sc_sb"], writes=["sc_sb"])
            s.add("V", lambda v: v.reciprocal(sc_sb[0:1, 2:3], sc_sb[0:1, 1:2]),
                  reads=["sc_sb"], writes=["sc_sb"])
            s.add("V", lambda v: v.memset(onesb[:], 1.0), writes=["onesb"])
            s.add("V", lambda v: v.tensor_scalar(biasb[:], bias_row[:], sc_sb[0:1, 2:3], None,
                                                 op0=ALU.mult),
                  reads=["bias_row", "sc_sb"], writes=["biasb"])
            s.add("T", lambda t: t.matmul(pRaw[0][:, 0:512], ones_f[0:1, 0:128],
                                          bias_row[0:1, 0:512],
                                          start=True, stop=True, skip_group_check=True),
                  reads=["ones_f", "bias_row"], writes=["pRaw0"])
            s.add("T", lambda t: t.matmul(pRaw[1][:, 0:512], ones_f[0:1, 0:128],
                                          bias_row[0:1, 512:1024],
                                          start=True, stop=True, skip_group_check=True),
                  reads=["ones_f", "bias_row"], writes=["pRaw1"])
            s.add("T", lambda t: t.matmul(pRaw[2][:, 0:1], ones_f[0:1, 0:128], sc_sb[0:1, 1:2],
                                          start=True, stop=True, skip_group_check=True),
                  reads=["ones_f", "sc_sb"], writes=["pRaw2"])
            s.add("T", lambda t: t.matmul(pRaw[2][:, 1:2], ones_f[0:1, 0:128], sc_sb[0:1, 2:3],
                                          start=True, stop=True, skip_group_check=True),
                  reads=["ones_f", "sc_sb"], writes=["pRaw2"])
            s.add("V", lambda v: v.tensor_copy(bias_bc[:, 0:512], pRaw[0][:, :]),
                  reads=["pRaw0"], writes=["bias_bc"])
            s.add("V", lambda v: v.tensor_copy(bias_bc[:, 512:1024], pRaw[1][:, :]),
                  reads=["pRaw1"], writes=["bias_bc"])
            s.add("V", lambda v: v.tensor_copy(sprime_bc[:], pRaw[2][:, 0:1]),
                  reads=["pRaw2"], writes=["sprime_bc"])
            s.add("V", lambda v: v.tensor_copy(rs_bc[:], pRaw[2][:, 1:2]),
                  reads=["pRaw2"], writes=["rs_bc"])

            s.add("G", lambda g: g.dma_start(out=wqkv_bf, in_=wqkv), writes=["wqkv_bf"])
            s.add("G", lambda g: g.dma_start(out=wo_bf, in_=wo_in), writes=["wo_bf"])
            for c in range(8):
                s.add("SC", (lambda c: lambda e: e.dma_start_transpose(
                    w_kvT[:, c * 2048:(c + 1) * 2048],
                    wqkv_bf[D:3 * D, c * 128:(c + 1) * 128]))(c),
                    reads=["wqkv_bf"], writes=["w_kvT"], dma=True)
                s.add("SC", (lambda c: lambda e: e.dma_start_transpose(
                    w_qT[:, c * 1024:(c + 1) * 1024],
                    wqkv_bf[0:D, c * 128:(c + 1) * 128]))(c),
                    reads=["wqkv_bf"], writes=["w_qT"], dma=True)
                s.add("SC", (lambda c: lambda e: e.dma_start_transpose(
                    w_oT[:, c * 1024:(c + 1) * 1024],
                    wo_bf[:, c * 128:(c + 1) * 128]))(c),
                    reads=["wo_bf"], writes=["w_oT"], dma=True)

            wjobs = []
            for ot in range(8):
                wjobs.append((tern, ot * 128, w_fT, "w_fT", 1024, 0, ot, True))

            for ji, (src, row, dst, dkey, blk, base, ot, is_t) in enumerate(wjobs):
                stg = w_st[ji % 2]
                skey = f"w_st{ji % 2}"
                wsb = ji % 2
                s.add("SY", (lambda src, row, wsb: lambda sy: sy.dma_start(
                    out=w_sti[wsb][:], in_=src[row:row + 128, :]))(src, row, wsb),
                    writes=[f"w_sti{wsb}"])
                s.add("SC", (lambda wsb: lambda sc: sc.copy(w_stf[:], w_sti[wsb][:]))(wsb),
                      reads=[f"w_sti{wsb}"], writes=["w_stf"])
                s.add("V", (lambda stg: lambda v: v.tensor_scalar(
                    stg[:], w_stf[:], -1.0, None, op0=ALU.add))(stg),
                    reads=["w_stf"], writes=[skey])
                pdst = pD if ji % 2 == 0 else pE
                pkey = "pD" if ji % 2 == 0 else "pE"
                for c in range(8):
                    s.add("T", (lambda pdst, stg, c: lambda t: t.transpose(
                        pdst[:, c * 128:(c + 1) * 128],
                        stg[:, c * 128:(c + 1) * 128], identb[:]))(pdst, stg, c),
                        reads=[skey, "identb"], writes=[pkey])
                dstv = dst[:].rearrange("p (ic c) -> p ic c", ic=8)[
                    :, :, base + ot * 128: base + (ot + 1) * 128]
                srcv = pdst[:, :].rearrange("p (ic c) -> p ic c", ic=8)
                s.add("V", (lambda dstv, srcv: lambda v: v.tensor_copy(dstv, srcv))(dstv, srcv),
                      reads=[pkey], writes=[dkey])

            def emit_xload(i):
                kind, idx = jobs[i]
                buf_i = i % 3
                src_ap = x_kv if kind == "K" else x_own
                xin = src_ap[idx * 512:(idx + 1) * 512, :].rearrange("(i p) c -> p i c", p=128)
                xdst = x_tm[buf_i][:].rearrange("p (i c) -> p i c", i=4)
                for h in range(2):
                    s.add("G", (lambda xin, xdst, h: lambda g: g.dma_start(
                        out=xdst[:, 2 * h:2 * h + 2, :], in_=xin[:, 2 * h:2 * h + 2, :]))(
                        xin, xdst, h), writes=[f"x_tm{buf_i}"])

            def emit_xtp_group(i, cp):
                buf_i = i % 3
                tb = i % 2
                pdst = pD if cp % 2 == 0 else pE
                pkey = "pD" if cp % 2 == 0 else "pE"
                for ch in range(2):
                    c = 2 * cp + ch
                    for i4 in range(4):
                        s.add("T", (lambda pdst, b, i4, c, ch: lambda t: t.transpose(
                            pdst[:, ch * 512 + i4 * 128: ch * 512 + (i4 + 1) * 128],
                            x_tm[b][:, i4 * 1024 + c * 128: i4 * 1024 + (c + 1) * 128],
                            identb[:]))(pdst, buf_i, i4, c, ch),
                            reads=[f"x_tm{buf_i}", "identb"], writes=[pkey])
                s.add("SC", (lambda pdst, cp, b: lambda sc: sc.copy(
                    x_T[b][:, cp * 1024:(cp + 1) * 1024], pdst[:, :]))(pdst, cp, tb),
                    reads=[pkey], writes=[f"x_T{tb}"])

            def emit_xtp(i, _unused):
                for cp in range(4):
                    emit_xtp_group(i, cp)

            def emit_proj_mm(wT, wkey, p, xb):
                pk = pRaw[p % 3]
                pkkey = f"pRaw{p % 3}"
                for ic in range(8):
                    s.add("T", (lambda pk, wT, p, ic, xb: lambda t: t.matmul(
                        pk[:, 0:512], wT[:, ic * 2048 + p * 128: ic * 2048 + (p + 1) * 128]
                        if wT is w_kvT else wT[:, ic * 1024 + p * 128: ic * 1024 + (p + 1) * 128],
                        x_T[xb][:, ic * 512:(ic + 1) * 512],
                        start=(ic == 0), stop=(ic == 7), skip_group_check=True))(pk, wT, p, ic, xb),
                        reads=[wkey, f"x_T{xb}"], writes=[pkkey])
                s.add("SC", (lambda pk, p: lambda sc: sc.copy(
                    kraw[p % 3][:], pk[:, 0:512]))(pk, p),
                    reads=[pkkey], writes=[f"kraw{p % 3}"])

            def emit_swap(p):
                s.add("T", (lambda p: lambda t: t.matmul(
                    pSwp[:, 0:512], pswapb[:], kraw[p % 3][:],
                    start=True, stop=True, skip_group_check=True))(p),
                    reads=["pswapb", f"kraw{p % 3}"], writes=["pSwp"])
                s.add("SC", (lambda p: lambda sc: sc.copy(
                    ksw[p % 2][:], pSwp[:, 0:512]))(p),
                    reads=["pSwp"], writes=[f"ksw{p % 2}"])

            def emit_rope_tail(p, cs, cskey, dst_fn, dst_key):
                krk = f"kraw{p % 3}"
                s.add("V", (lambda cs, p: lambda v: v.tensor_tensor(
                    t2b[:], kraw[p % 3][:], cs[:, 0:512], ALU.mult))(cs, p),
                    reads=[krk, cskey], writes=["t2b"])
                s.add("V", (lambda p, cs: lambda v: v.tensor_tensor(
                    t1b[:], ksw[p % 2][:], cs[:, 512:1024], ALU.mult))(p, cs),
                    reads=[f"ksw{p % 2}", cskey], writes=["t1b"])
                s.add("V", (lambda p: lambda v: v.tensor_tensor(
                    krot[p % 2][:], t1b[:], t2b[:], ALU.add))(p),
                    reads=["t1b", "t2b"], writes=[f"krot{p % 2}"])
                s.add("SY", dst_fn, reads=[f"krot{p % 2}"], writes=[dst_key])

            for vb in range(2):
                onesv0 = v_stg[vb][:].rearrange("p (j e k) -> p j e k", j=8, e=2)[
                    :, :, :, 64:128]
                s.add("V", (lambda onesv0: lambda v: v.memset(onesv0, 1.0))(onesv0),
                      reads=[], writes=[f"v_stg{vb}"])

            jobs = [("K", i) for i in range(8)] + [("Q", i) for i in range(4)]

            def emit_cs(kind, idx, bi):
                csl = cs_b[bi]
                cskey = f"cs_b{bi}"
                csrc, ssrc = (cosb, sinb) if kind == "K" else (cosq, sinq)
                s.add("G", (lambda idx, csl, csrc: lambda g: g.dma_start(
                    out=csl[:, 0:512], in_=csrc[:, idx * 512:(idx + 1) * 512]))(idx, csl, csrc),
                    writes=[cskey])
                s.add("G", (lambda idx, csl, ssrc: lambda g: g.dma_start(
                    out=csl[:, 512:1024], in_=ssrc[:, idx * 512:(idx + 1) * 512]))(idx, csl, ssrc),
                    writes=[cskey])


            def emit_vproj(t8, i4, bi):
                for half in range(2):
                    hp = (i4 * 2 + half) % 2
                    pk = pC[hp]
                    pkkey = f"pC{hp}"
                    for ic in range(8):
                        s.add("T", (lambda i4, ic, half, b, pk: lambda t: t.matmul(
                            pk[:, 0:512],
                            x_T[b][:, ic * 512 + i4 * 128: ic * 512 + (i4 + 1) * 128],
                            w_kvT[:, ic * 2048 + 1024 + half * 512:
                                  ic * 2048 + 1024 + (half + 1) * 512],
                            start=(ic == 0), stop=(ic == 7),
                            skip_group_check=True))(i4, ic, half, bi, pk),
                            reads=[f"x_T{bi}", "w_kvT"], writes=[pkkey])
                    kt = t8 * 4 + i4
                    srcv = pk[:, 0:512].rearrange("p (j e c) -> p j e c", j=4, e=2)
                    dstv = v_stg[kt % 2][:].rearrange("p (j e k) -> p j e k", j=8, e=2)[
                        :, half * 4:(half + 1) * 4, :, 0:64]
                    s.add("SC", (lambda dstv, srcv: lambda sc: sc.copy(dstv, srcv))(
                        dstv, srcv), reads=[pkkey], writes=[f"v_stg{kt % 2}"])
                kt = t8 * 4 + i4
                vout = v_scr[:, :, kt, :].rearrange("j p c -> p j c")
                vsrc = v_stg[kt % 2][:].rearrange("p (j c) -> p j c", j=8)
                s.add("SY", (lambda vout, vsrc: lambda sy: sy.dma_start(
                    out=vout, in_=vsrc))(vout, vsrc),
                    reads=[f"v_stg{kt % 2}"], writes=["v_scr"])

            def emit_proc(i, nxt):
                kind, idx = jobs[i]
                bi = i % 2
                csl = cs_b[bi]
                cskey = f"cs_b{bi}"
                if kind == "K":
                    t8 = idx

                    def dst(p):
                        return lambda sy: sy.dma_start(
                            out=k_scr[p, :, t8 * 512:(t8 + 1) * 512], in_=krot[p % 2][:])
                    wT, wkey, dkey = w_kvT, "w_kvT", "k_scr"
                else:
                    u = idx

                    def dst(p):
                        return lambda sy: sy.dma_start(
                            out=q_scr[p, :, u * 512:(u + 1) * 512], in_=krot[p % 2][:])
                    wT, wkey, dkey = w_qT, "w_qT", "q_scr"
                for p in range(8):
                    emit_proj_mm(wT, wkey, p, bi)
                    if nxt is not None and p % 2 == 1:
                        emit_xtp_group(nxt, p // 2)
                    if p >= 1:
                        emit_swap(p - 1)
                    if p >= 2:
                        emit_rope_tail(p - 2, csl, cskey, dst(p - 2), dkey)
                    if kind == "K" and p % 2 == 1:
                        emit_vproj(t8, p // 2, bi)
                emit_swap(7)
                emit_rope_tail(6, csl, cskey, dst(6), dkey)
                emit_rope_tail(7, csl, cskey, dst(7), dkey)

            emit_xload(0)
            emit_cs(*jobs[0], 0)
            emit_xload(1)
            emit_xtp(0, None)
            for i in range(12):
                if i + 2 < 12:
                    emit_xload(i + 2)
                if i + 1 < 12:
                    emit_cs(*jobs[i + 1], (i + 1) % 2)
                    emit_proc(i, i + 1)
                else:
                    emit_proc(i, None)

            bar = barrier_counts()
            s.run(block, {})

        with contextlib.ExitStack() as ph:
            sbp = lambda name, shape, dt: ph.enter_context(nc.sbuf_tensor(name, shape, dt))
            psp = lambda name, shape, dt: ph.enter_context(nc.psum_tensor(name, shape, dt))
            maskb = sbp("maskb", [128, 2 * 8 * 512], BF16)
            k_run = [sbp(f"k_run{i}", [128, 4096], BF16) for i in range(3)]
            v_run = [sbp(f"v_run{i}", [128, NKT * 256], BF16) for i in range(3)]
            q_run = [sbp(f"q_run{i}", [128, 512], BF16) for i in range(3)]
            exp_b = [sbp(f"exp_b{i}", [128, 1024], BF16) for i in range(3)]
            recb = sbp("recb", [128, 1024], F32)
            pSC = [psp(f"pSC{i}", [128, 1024], F32) for i in range(2)]
            pAV = [psp(f"pAV{i}", [128, 1024], F32) for i in range(2)]

            s = Sched(CH, KS)
            mdst = maskb[:].rearrange("p (a b c) -> p a b c", a=2, b=8)
            s.add("G", (lambda mdst: lambda g: g.dma_start(
                out=mdst, in_=masks.rearrange("a b p c -> p a b c")))(mdst),
                writes=["maskb"])

            iters = [(sl, j) for sl in range(NSLOT) for j in range(NPAIR)]
            pending_tail = None
            for it, (sl, j) in enumerate(iters):
                n = NS[sl]
                bi = it % 3
                kk, vk, qk = f"k_run{bi}", f"v_run{bi}", f"q_run{bi}"
                s.add("SY", (lambda j, n, bi: lambda sy: sy.dma_start(
                    out=k_run[bi][:, 0:n * 128], in_=k_scr[j, :, 0:n * 128]))(j, n, bi),
                    reads=["k_scr"], writes=[kk])
                vin = v_scr[j, :, 0:n, :]
                vdst = v_run[bi][:, 0:n * 256].rearrange("p (a c) -> p a c", a=n)
                s.add("SY", (lambda vin, vdst: lambda sy: sy.dma_start(
                    out=vdst, in_=vin))(vin, vdst), reads=["v_scr"], writes=[vk])
                s.add("SY", (lambda j, sl, bi: lambda sy: sy.dma_start(
                    out=q_run[bi][:], in_=q_scr[j, :, sl * 512:(sl + 1) * 512]))(j, sl, bi),
                    reads=["q_scr"], writes=[qk])
                def emit_scores(t):
                    sb_ = t % 2
                    pk = f"pSC{sb_}"
                    s.add("T", (lambda sb_, bi, t: lambda te: te.matmul(
                        pSC[sb_][:, 0:512], k_run[bi][0:64, t * 128:(t + 1) * 128],
                        q_run[bi][0:64, :], start=True, stop=True,
                        tile_position=(0, 0), skip_group_check=True))(sb_, bi, t),
                        reads=[kk, qk], writes=[pk])
                    s.add("T", (lambda sb_, bi, t: lambda te: te.matmul(
                        pSC[sb_][:, 512:1024],
                        k_run[bi][64:128, t * 128:(t + 1) * 128],
                        q_run[bi][64:128, :], start=True, stop=True,
                        tile_position=(64, 0), skip_group_check=True))(sb_, bi, t),
                        reads=[kk, qk], writes=[pk])

                def emit_exp(t):
                    eb = t % 3
                    sb_ = t % 2
                    s.add("SC", (lambda eb, sb_: lambda sc: sc.activation(
                        exp_b[eb][:, :], pSC[sb_][:, :], ACTF.Exp))(eb, sb_),
                        reads=[f"pSC{sb_}"], writes=[f"exp_b{eb}"])

                def emit_mask(t):
                    if t < n - 8:
                        return
                    eb = t % 3
                    w = t - (n - 8)
                    moff = ((sl % 2) * 8 + w) * 512
                    for half in range(2):
                        s.add("V", (lambda eb, moff, half: lambda v: v.tensor_tensor(
                            exp_b[eb][:, half * 512:(half + 1) * 512],
                            exp_b[eb][:, half * 512:(half + 1) * 512],
                            maskb[:, moff:moff + 512], ALU.mult))(eb, moff, half),
                            reads=[f"exp_b{eb}", "maskb"], writes=[f"exp_b{eb}"])

                def emit_av(t):
                    eb = t % 3
                    ek = f"exp_b{eb}"
                    pit = it % 2
                    for hh in range(2):
                        s.add("T", (lambda eb, bi, t, hh, n, pit: lambda te: te.matmul(
                            pAV[pit][:, hh * 512:(hh + 1) * 512],
                            v_run[bi][:, t * 256 + hh * 128: t * 256 + (hh + 1) * 128],
                            exp_b[eb][:, hh * 512:(hh + 1) * 512],
                            start=(t == 0), stop=(t == n - 1),
                            skip_group_check=True))(eb, bi, t, hh, n, pit),
                            reads=[ek, vk], writes=[f"pAV{pit}"])

                def emit_tail(psl, pj, pit):
                    s.add("SC", (lambda pit: lambda sc: sc.activation(
                        recb[64:128, :], pAV[pit][64:128, :], ACTF.Ln))(pit),
                        reads=[f"pAV{pit}"], writes=["recb"])
                    s.add("SC", lambda sc: sc.activation(
                        recb[64:128, :], recb[64:128, :], ACTF.Exp, scale=-1.0),
                        reads=["recb"], writes=["recb"])
                    dstA = attn_n[0:64, pj * 2048 + psl * 512: pj * 2048 + (psl + 1) * 512]
                    s.add("V", (lambda dstA, pit: lambda v: v.tensor_tensor(
                        dstA, pAV[pit][0:64, 0:512], recb[64:128, 0:512], ALU.mult))(dstA, pit),
                        reads=[f"pAV{pit}", "recb"], writes=["attn_n"])
                    dstB = attn_n[64:128, pj * 2048 + psl * 512: pj * 2048 + (psl + 1) * 512]
                    s.add("V", (lambda dstB, pit: lambda v: v.tensor_tensor(
                        dstB, pAV[pit][0:64, 512:1024], recb[64:128, 512:1024],
                        ALU.mult))(dstB, pit),
                        reads=[f"pAV{pit}", "recb"], writes=["attn_n"])

                for t in range(n):
                    emit_scores(t)
                    emit_exp(t)
                    emit_mask(t)
                    if t >= 2:
                        emit_av(t - 2)
                    if t == 4 and pending_tail is not None:
                        emit_tail(*pending_tail)
                        pending_tail = None
                emit_av(n - 2)
                emit_av(n - 1)
                if pending_tail is not None:
                    emit_tail(*pending_tail)
                    pending_tail = None
                pending_tail = (sl, j, it % 2)

            if pending_tail is not None:
                emit_tail(*pending_tail)
                pending_tail = None
            s.run(block, bar)
            bar = barrier_counts()

        with contextlib.ExitStack() as ph:
            sbp = lambda name, shape, dt: ph.enter_context(nc.sbuf_tensor(name, shape, dt))
            psp = lambda name, shape, dt: ph.enter_context(nc.psum_tensor(name, shape, dt))
            ND = 4
            xo = [sbp(f"xo{i}", [128, 1024], F32) for i in range(ND)]
            y1f = [sbp(f"y1f{i}", [128, 1024], F32) for i in range(ND)]
            ylnf = [sbp(f"ylnf{i}", [128, 1024], F32) for i in range(ND)]
            uf = [sbp(f"uf{i}", [128, 1024], F32) for i in range(ND)]
            uf2 = [sbp(f"uf2{i}", [128, 1024], F32) for i in range(ND)]
            ui32 = [sbp(f"ui32{i}", [128, 1024], I32) for i in range(2)]
            rdl = [sbp(f"rdl{i}", [128, 1024], F32) for i in range(ND)]
            yout = [sbp(f"yout{i}", [128, 1024], F32) for i in range(2)]
            sqb = [sbp("sqb0", [128, 1024], F32)] * 2
            xq_bf = [sbp(f"xq_bf{i}", [128, 1024], BF16) for i in range(ND)]
            xq_T = [sbp(f"xq_T{i}", [128, 1024], BF16) for i in range(ND)]
            mst = sbp("mst", [128, 16 * ND], F32)
            pR = [psp(f"pR{i}", [128, 1024], F32) for i in range(3)]
            pT = psp("pT", [128, 1024], BF16)

            s = Sched(CH, KS)

            def ln_stats(src_t, skey, c0, bi):
                s.add("SC", (lambda src_t, c0, bi: lambda sc: sc.activation(
                    sqb[bi % 2][:], src_t[:], ACTF.Identity, scale=1.0 / D,
                    accum_out=mst[:, c0 + 2:c0 + 3]))(src_t, c0, bi),
                    reads=[skey], writes=["sqb", f"mst{bi}"])
                s.add("SC", (lambda src_t, c0, bi: lambda sc: sc.activation(
                    sqb[bi % 2][:], src_t[:], ACTF.Square, scale=1.0 / float(np.sqrt(D)),
                    accum_out=mst[:, c0 + 3:c0 + 4]))(src_t, c0, bi),
                    reads=[skey], writes=["sqb", f"mst{bi}"])
                s.add("V", (lambda c0: lambda v: v.scalar_tensor_tensor(
                    mst[:, c0 + 4:c0 + 5], mst[:, c0 + 2:c0 + 3], mst[:, c0 + 2:c0 + 3],
                    mst[:, c0 + 3:c0 + 4], op0=ALU.mult, op1=ALU.subtract))(c0),
                    reads=[f"mst{bi}"], writes=[f"mst{bi}"])
                s.add("V", (lambda c0: lambda v: v.tensor_scalar(
                    mst[:, c0 + 4:c0 + 5], mst[:, c0 + 4:c0 + 5], -1.0, EPS,
                    op0=ALU.mult, op1=ALU.add))(c0), reads=[f"mst{bi}"], writes=[f"mst{bi}"])
                s.add("SC", (lambda c0: lambda sc: sc.activation(
                    mst[:, c0 + 5:c0 + 6], mst[:, c0 + 4:c0 + 5], ACTF.Ln))(c0),
                    reads=[f"mst{bi}"], writes=[f"mst{bi}"])
                s.add("SC", (lambda c0: lambda sc: sc.activation(
                    mst[:, c0 + 6:c0 + 7], mst[:, c0 + 5:c0 + 6], ACTF.Exp, scale=-0.5))(c0),
                    reads=[f"mst{bi}"], writes=[f"mst{bi}"])
                s.add("V", (lambda c0: lambda v: v.tensor_scalar(
                    mst[:, c0 + 7:c0 + 8], mst[:, c0 + 2:c0 + 3], mst[:, c0 + 6:c0 + 7], -1.0,
                    op0=ALU.mult, op1=ALU.mult))(c0), reads=[f"mst{bi}"], writes=[f"mst{bi}"])

            def emit_stage(g16, st):
                sl, i4 = g16 // 4, g16 % 4
                bi = g16 % ND
                moff = 16 * bi
                pw = g16 % 3
                pf = g16 % 3
                if st == 0:
                    s.add("SY", (lambda g16, bi: lambda sy: sy.dma_start(
                        out=xo[bi][:], in_=x_own[g16 * 128:(g16 + 1) * 128, :]))(g16, bi),
                        writes=[f"xo{bi}"])
                    for dc in range(8):
                        for half in range(2):
                            s.add("T", (lambda sl, i4, dc, half, pw: lambda te: te.matmul(
                                pR[pw][:, half * 512:(half + 1) * 512],
                                attn_n[:, dc * 2048 + sl * 512 + i4 * 128:
                                       dc * 2048 + sl * 512 + (i4 + 1) * 128],
                                w_oT[:, dc * 1024 + half * 512: dc * 1024 + (half + 1) * 512],
                                start=(dc == 0), stop=(dc == 7),
                                skip_group_check=True))(sl, i4, dc, half, pw),
                                reads=["attn_n", "w_oT"], writes=[f"pR{pw}"])
                elif st == 1:
                    s.add("V", (lambda bi, pw: lambda v: v.tensor_tensor(
                        y1f[bi][:], pR[pw][:, :], xo[bi][:], ALU.add))(bi, pw),
                        reads=[f"pR{pw}", f"xo{bi}"], writes=[f"y1f{bi}"])
                elif st == 2:
                    s.add("SC", (lambda bi, moff: lambda sc: sc.activation(
                        sqb[bi % 2][:], y1f[bi][:], ACTF.Identity, scale=1.0 / D,
                        accum_out=mst[:, moff + 2:moff + 3]))(bi, moff),
                        reads=[f"y1f{bi}"], writes=["sqb", f"mst{bi}"])
                    s.add("SC", (lambda bi, moff: lambda sc: sc.activation(
                        sqb[bi % 2][:], y1f[bi][:], ACTF.Square, scale=1.0 / float(np.sqrt(D)),
                        accum_out=mst[:, moff + 3:moff + 4]))(bi, moff),
                        reads=[f"y1f{bi}"], writes=["sqb", f"mst{bi}"])
                elif st == 3:
                    s.add("V", (lambda bi, moff: lambda v: v.scalar_tensor_tensor(
                        mst[:, moff + 4:moff + 5], mst[:, moff + 2:moff + 3],
                        mst[:, moff + 2:moff + 3], mst[:, moff + 3:moff + 4],
                        op0=ALU.mult, op1=ALU.subtract))(bi, moff),
                        reads=[f"mst{bi}"], writes=[f"mst{bi}"])
                    s.add("V", (lambda bi, moff: lambda v: v.tensor_scalar(
                        mst[:, moff + 4:moff + 5], mst[:, moff + 4:moff + 5], -1.0, EPS,
                        op0=ALU.mult, op1=ALU.add))(bi, moff),
                        reads=[f"mst{bi}"], writes=[f"mst{bi}"])
                elif st == 4:
                    s.add("SC", (lambda bi, moff: lambda sc: sc.activation(
                        mst[:, moff + 5:moff + 6], mst[:, moff + 4:moff + 5],
                        ACTF.Ln))(bi, moff),
                        reads=[f"mst{bi}"], writes=[f"mst{bi}"])
                    s.add("SC", (lambda bi, moff: lambda sc: sc.activation(
                        mst[:, moff + 6:moff + 7], mst[:, moff + 5:moff + 6],
                        ACTF.Exp, scale=-0.5))(bi, moff),
                        reads=[f"mst{bi}"], writes=[f"mst{bi}"])
                elif st == 5:
                    s.add("V", (lambda bi, moff: lambda v: v.tensor_scalar(
                        mst[:, moff + 7:moff + 8], mst[:, moff + 2:moff + 3],
                        mst[:, moff + 6:moff + 7], -1.0,
                        op0=ALU.mult, op1=ALU.mult))(bi, moff),
                        reads=[f"mst{bi}"], writes=[f"mst{bi}"])
                    s.add("V", (lambda bi, moff: lambda v: v.tensor_scalar(
                        ylnf[bi][:], y1f[bi][:], mst[:, moff + 6:moff + 7],
                        mst[:, moff + 7:moff + 8], op0=ALU.mult, op1=ALU.add))(bi, moff),
                        reads=[f"y1f{bi}", f"mst{bi}"], writes=[f"ylnf{bi}"])
                    s.add("V", (lambda bi: lambda v: v.tensor_scalar(
                        uf[bi][:], ylnf[bi][:], rs_bc[:, 0:1], None, op0=ALU.mult))(bi),
                        reads=[f"ylnf{bi}", "rs_bc"], writes=[f"uf{bi}"])
                elif st == 6:
                    s.add("SC", (lambda bi: lambda sc: sc.copy(ui32[bi % 2][:], uf[bi][:]))(bi),
                          reads=[f"uf{bi}"], writes=[f"ui32{bi % 2}"])
                    s.add("SC", (lambda bi: lambda sc: sc.copy(uf2[bi][:], ui32[bi % 2][:]))(bi),
                          reads=[f"ui32{bi % 2}"], writes=[f"uf2{bi}"])
                elif st == 7:
                    s.add("V", (lambda bi: lambda v: v.tensor_tensor(
                        rdl[bi][:], uf[bi][:], uf2[bi][:], ALU.subtract))(bi),
                        reads=[f"uf{bi}", f"uf2{bi}"], writes=[f"rdl{bi}"])
                    s.add("V", (lambda bi: lambda v: v.scalar_tensor_tensor(
                        uf[bi][:], rdl[bi][:], 0.5, uf2[bi][:], op0=ALU.is_ge,
                        op1=ALU.add))(bi),
                        reads=[f"rdl{bi}", f"uf2{bi}"], writes=[f"uf{bi}"])
                    s.add("V", (lambda bi: lambda v: v.tensor_scalar(
                        rdl[bi][:], rdl[bi][:], -0.5, -1.0, op0=ALU.is_le,
                        op1=ALU.mult))(bi),
                        reads=[f"rdl{bi}"], writes=[f"rdl{bi}"])
                    s.add("V", (lambda bi: lambda v: v.tensor_tensor(
                        uf[bi][:], uf[bi][:], rdl[bi][:], ALU.add))(bi),
                        reads=[f"uf{bi}", f"rdl{bi}"], writes=[f"uf{bi}"])
                    s.add("V", (lambda bi: lambda v: v.tensor_scalar(
                        xq_bf[bi][:], uf[bi][:], QB, -QB, op0=ALU.min, op1=ALU.max))(bi),
                        reads=[f"uf{bi}"], writes=[f"xq_bf{bi}"])
                elif st == 8:
                    for c in range(8):
                        s.add("T", (lambda c, bi: lambda te: te.transpose(
                            pT[:, c * 128:(c + 1) * 128], xq_bf[bi][:, c * 128:(c + 1) * 128],
                            identb[:]))(c, bi),
                            reads=[f"xq_bf{bi}", "identb"], writes=["pT"])
                    s.add("SC", (lambda bi: lambda sc: sc.copy(xq_T[bi][:], pT[:, :]))(bi),
                          reads=["pT"], writes=[f"xq_T{bi}"])
                elif st == 9:
                    for ic in range(8):
                        for half in range(2):
                            s.add("T", (lambda ic, half, bi, pf: lambda te: te.matmul(
                                pR[pf][:, half * 512:(half + 1) * 512],
                                xq_T[bi][:, ic * 128:(ic + 1) * 128],
                                w_fT[:, ic * 1024 + half * 512: ic * 1024 + (half + 1) * 512],
                                start=(ic == 0), stop=False,
                                skip_group_check=True))(ic, half, bi, pf),
                                reads=[f"xq_T{bi}", "w_fT"], writes=[f"pR{pf}"])
                    for half in range(2):
                        s.add("T", (lambda half, pf: lambda te: te.matmul(
                            pR[pf][:, half * 512:(half + 1) * 512],
                            onesb[0:1, :], biasb[0:1, half * 512:(half + 1) * 512],
                            start=False, stop=True, skip_group_check=True))(half, pf),
                            reads=["onesb", "biasb"], writes=[f"pR{pf}"])
                elif st == 10:
                    s.add("V", (lambda bi, pf: lambda v: v.scalar_tensor_tensor(
                        y1f[bi][:], pR[pf][:, :], sprime_bc[:, 0:1], ylnf[bi][:],
                        op0=ALU.mult, op1=ALU.add))(bi, pf),
                        reads=[f"pR{pf}", "sprime_bc", f"ylnf{bi}"], writes=[f"y1f{bi}"])
                elif st == 11:
                    s.add("SC", (lambda bi, moff: lambda sc: sc.activation(
                        sqb[bi % 2][:], y1f[bi][:], ACTF.Identity, scale=1.0 / D,
                        accum_out=mst[:, moff + 10:moff + 11]))(bi, moff),
                        reads=[f"y1f{bi}"], writes=["sqb", f"mst{bi}"])
                    s.add("SC", (lambda bi, moff: lambda sc: sc.activation(
                        sqb[bi % 2][:], y1f[bi][:], ACTF.Square, scale=1.0 / float(np.sqrt(D)),
                        accum_out=mst[:, moff + 11:moff + 12]))(bi, moff),
                        reads=[f"y1f{bi}"], writes=["sqb", f"mst{bi}"])
                elif st == 12:
                    s.add("V", (lambda bi, moff: lambda v: v.scalar_tensor_tensor(
                        mst[:, moff + 12:moff + 13], mst[:, moff + 10:moff + 11],
                        mst[:, moff + 10:moff + 11], mst[:, moff + 11:moff + 12],
                        op0=ALU.mult, op1=ALU.subtract))(bi, moff),
                        reads=[f"mst{bi}"], writes=[f"mst{bi}"])
                    s.add("V", (lambda bi, moff: lambda v: v.tensor_scalar(
                        mst[:, moff + 12:moff + 13], mst[:, moff + 12:moff + 13], -1.0, EPS,
                        op0=ALU.mult, op1=ALU.add))(bi, moff),
                        reads=[f"mst{bi}"], writes=[f"mst{bi}"])
                elif st == 13:
                    s.add("SC", (lambda bi, moff: lambda sc: sc.activation(
                        mst[:, moff + 13:moff + 14], mst[:, moff + 12:moff + 13],
                        ACTF.Ln))(bi, moff),
                        reads=[f"mst{bi}"], writes=[f"mst{bi}"])
                    s.add("SC", (lambda bi, moff: lambda sc: sc.activation(
                        mst[:, moff + 14:moff + 15], mst[:, moff + 13:moff + 14],
                        ACTF.Exp, scale=-0.5))(bi, moff),
                        reads=[f"mst{bi}"], writes=[f"mst{bi}"])
                elif st == 14:
                    s.add("V", (lambda bi, moff: lambda v: v.tensor_scalar(
                        mst[:, moff + 15:moff + 16], mst[:, moff + 10:moff + 11],
                        mst[:, moff + 14:moff + 15], -1.0,
                        op0=ALU.mult, op1=ALU.mult))(bi, moff),
                        reads=[f"mst{bi}"], writes=[f"mst{bi}"])
                    s.add("V", (lambda bi, moff: lambda v: v.tensor_scalar(
                        yout[bi % 2][:], y1f[bi][:], mst[:, moff + 14:moff + 15],
                        mst[:, moff + 15:moff + 16], op0=ALU.mult, op1=ALU.add))(bi, moff),
                        reads=[f"y1f{bi}", f"mst{bi}"], writes=[f"yout{bi % 2}"])
                elif st == 15:
                    s.add("SY", (lambda g16, bi: lambda sy: sy.dma_start(
                        out=out[g16 * 128:(g16 + 1) * 128, :], in_=yout[bi % 2][:]))(g16, bi),
                        reads=[f"yout{bi % 2}"], writes=["out"])

            NSTAGE, OFF = 16, 4
            for k in range(NSTAGE + OFF * 15 + 1):
                for g16 in range(16):
                    st = k - OFF * g16
                    if 0 <= st < NSTAGE:
                        emit_stage(g16, st)

            s.run(block, bar)

            @block.sync
            def _(sy):
                sy.wait_ge(KS["out"].h, KS["out"].n)

    return nc


def _host_tables(cls):
    j = np.arange(128) % 32
    invf = ROPE_THETA ** (-(j.astype(np.float64)) / 32.0)
    srange = np.arange(S, dtype=np.float64)
    ang = srange[None, :] * invf[:, None]
    amp = 1.0 / np.sqrt(np.sqrt(64.0))
    cos_t = (np.cos(ang) * amp).astype(np.float32)
    sgn = np.where((np.arange(128) % 64) < 32, -1.0, 1.0) * amp
    sin_t = (np.sin(ang) * sgn[:, None]).astype(np.float32)
    pos = POS[cls]
    qcols = np.concatenate([np.arange(512 * p, 512 * (p + 1)) for p in pos])
    cosq_t = cos_t[:, qcols].copy()
    sinq_t = sin_t[:, qcols].copy()
    kk = np.arange(128)[:, None]
    qq = np.arange(512)[None, :]
    tri = [(kk + 128 * d <= qq).astype(np.float32) for d in range(4)]
    O = np.ones((128, 512), np.float32)
    Z = np.zeros((128, 512), np.float32)
    OT = np.stack([O, O, O, O] + tri)
    TZ = np.stack(tri + [Z, Z, Z, Z])
    if cls == 0:
        mk = np.stack([OT, TZ])
    else:
        mk = np.stack([TZ, OT])
    return cos_t, sin_t, cosq_t, sinq_t, mk


def kernel(L0_attn_x, L0_attn_W_qkv, L0_attn_W_o, L0_ffn_ternary_weight,
           L0_ffn_bias, L0_ffn_act_scale):
    x = np.ascontiguousarray(np.asarray(L0_attn_x, dtype=np.float32))
    wq = np.asarray(L0_attn_W_qkv, dtype=np.float32)
    wo = np.ascontiguousarray(np.asarray(L0_attn_W_o, dtype=np.float32))
    tw = np.ascontiguousarray(np.asarray(L0_ffn_ternary_weight, dtype=np.int32))
    bb = np.asarray(L0_ffn_bias, dtype=np.float32).reshape(1, D)
    asc = np.asarray(L0_ffn_act_scale, dtype=np.float32).reshape(1, 1)

    perm = np.arange(3 * D)
    for blk in range(2):
        for h in range(H):
            for e in range(2):
                for jj in range(32):
                    perm[blk * D + h * 64 + e * 32 + jj] = blk * D + h * 64 + 2 * jj + e
    wqp = np.ascontiguousarray(wq[perm])

    pswap_m = np.zeros((128, 128), np.float32)
    for h2 in range(2):
        for jj in range(32):
            pswap_m[h2 * 64 + jj, h2 * 64 + 32 + jj] = 1.0
            pswap_m[h2 * 64 + 32 + jj, h2 * 64 + jj] = 1.0
    ident = np.eye(128, dtype=np.float32)

    nc = build_nc()
    in_maps = []
    metas = []
    for core in range(8):
        b, cls = core // 2, core % 2
        cos_t, sin_t, cosq_t, sinq_t, mk = _host_tables(cls)
        pos = POS[cls]
        x_ownc = np.ascontiguousarray(
            np.concatenate([x[b, 512 * p:512 * (p + 1), :] for p in pos], axis=0))
        in_maps.append({
            "x_kv": np.ascontiguousarray(x[b]),
            "x_own": x_ownc,
            "wqkv": wqp,
            "wo_in": wo,
            "tern": tw,
            "bias_in": bb,
            "ascale": asc,
            "cosb": cos_t,
            "sinb": sin_t,
            "cosq": np.ascontiguousarray(cosq_t),
            "sinq": np.ascontiguousarray(sinq_t),
            "pswap": pswap_m,
            "identm": ident,
            "masks": np.ascontiguousarray(mk),
        })
        metas.append((b, pos))

    res = run_bass_kernel_spmd(nc, in_maps, core_ids=list(range(8)))
    outp = np.empty((B, S, D), dtype=np.float32)
    for core in range(8):
        b, pos = metas[core]
        oc = res.results[core]["out"]
        for si, p in enumerate(pos):
            outp[b, 512 * p:512 * (p + 1), :] = oc[512 * si:512 * (si + 1), :]
    return outp
